# version 25
# baseline (speedup 1.0000x reference)
import contextlib
from collections import defaultdict

import numpy as np
import ml_dtypes

import concourse.bass as bass
import concourse.mybir as mybir
from concourse.bass_utils import run_bass_kernel_spmd

F32 = mybir.dt.float32
BF16 = mybir.dt.bfloat16
AF = mybir.ActivationFunctionType
ALU = mybir.AluOpType

D = 1024
H = 3072
T = 512
NT = 8
NK = D // 128
NHC = H // 128
NGU = H // 256
WARM = 256
QINV = 1.0 / 127.0
NSETUP_G = 16 * 2
NSETUP_A = 16 * 3
NSETUP_S = 16 * 5


class Sched:

    def __init__(self):
        self.recording = True
        self.cnt = defaultdict(int)
        self.marks = {}
        self.ops = defaultdict(list)

    def op(self, engine, make, inc=None, n=1, mark=None):
        if self.recording:
            if inc is not None:
                self.cnt[inc] += n
            if mark is not None:
                assert mark not in self.marks, f"dup mark {mark}"
                assert inc is not None, f"mark {mark} needs inc"
                self.marks[mark] = (inc, self.cnt[inc])
        else:
            self.ops[engine].append(("op", make, inc, n))

    def wait(self, engine, sem=None, mark=None, value=None):
        if self.recording:
            return
        if mark is not None:
            msem, v = self.marks[mark]
            assert sem is None or sem == msem, f"sem mismatch at {mark}"
            sem = msem
        else:
            v = value
        self.ops[engine].append(("wait", sem, v))

    def drain(self, engine):
        self.op(engine, lambda eng: eng.drain())


def build_nc():
    nc = bass.Bass("TRN2", target_bir_lowering=False)

    di = lambda name, sh, dt=F32: nc.dram_tensor(name, sh, dt, kind="ExternalInput").ap()
    xe = di("xe", [NT * T + WARM, D])
    win_t = di("win_t", [D, D])
    wout_t = di("wout_t", [D, D])
    wgu_t = di("wgu_t", [D, 2 * H], BF16)
    wd_t = di("wd_t", [H, D], BF16)
    ll_h = di("ll", [128, NK])
    gam_h = di("gam", [128, NK])
    bin_h = di("bin", [128, NK])
    bout_h = di("bout", [1, D])
    mask_h = di("mask", [128, 1])
    ident_h = di("ident", [128, 128])
    ones_h = di("ones1", [1, 128], BF16)
    out_h = nc.dram_tensor("out", [NT * T, D], F32, kind="ExternalOutput").ap()

    ctx = contextlib.ExitStack()
    sb = lambda name, sh, dt: ctx.enter_context(nc.sbuf_tensor(name, sh, dt))

    winT = sb("winT", [128, NK * D], BF16)
    woutT = sb("woutT", [128, NK * D], BF16)
    wgubuf = [sb(f"wgubuf{i}", [128, NK * 512], BF16) for i in range(3)]
    wdfull = sb("wdfull", [128, NHC * D], BF16)
    xbuf = [sb(f"xbuf{i}", [128, 4 * D], F32) for i in range(2)]
    xnb = sb("xnb", [128, 4 * D], F32)
    xnbT = sb("xnbT", [128, NK * T], BF16)
    guT = sb("guT", [128, NK * T], BF16)
    hTb = guT
    carry = sb("carry", [128, NK], F32)
    s_sb = sb("s_sb", [128, 4 * D], BF16)
    hbT = sb("hbT", [128, NHC * T], BF16)
    gslab = [sb(f"gslab{i}", [128, T], BF16) for i in range(4)]
    siluT = gslab[:2]
    stats = sb("stats", [128, 48], F32)
    mv = sb("mv", [128, 8], F32)
    sq4 = sb("sq4", [128, 4], F32)
    rstd4 = sb("rstd4", [128, 4], F32)
    negmr4 = sb("negmr4", [128, 4], F32)
    lam_sb = sb("lam_sb", [128, NK], F32)
    ll_sb = sb("ll_sb", [128, NK], F32)
    gam_sb = sb("gam_sb", [128, NK], F32)
    bin_sb = sb("bin_sb", [128, NK], F32)
    bgam_sb = sb("bgam_sb", [128, NK], F32)
    bout_bf = sb("bout_bf", [1, D], BF16)
    mask_sb = sb("mask_sb", [128, 1], F32)
    ident_sb = sb("ident_sb", [128, 128], F32)
    ones_sb = sb("ones_sb", [1, 128], BF16)
    bout_sb = xnb[0:1, 3 * D:4 * D]
    eps_h = di_eps = nc.dram_tensor("epsv", [128, 1], F32,
                                    kind="ExternalInput").ap()
    eps_sb = sb("eps_sb", [128, 1], F32)
    dummy_sb = sb("dummy_sb", [128, 1], F32)

    pb = [ctx.enter_context(nc.psum_tensor(f"pb{i}", [128, 512], F32))
          for i in range(8)]

    sems = {}
    for s in ("s_x0", "s_x1", "s_out0", "s_out1", "s_wgu0", "s_wgu1",
              "s_wdf", "s_setg", "s_sets", "s_t", "s_v", "s_a", "s_g",
              "s_xh0", "s_xh1", "s_xh2", "s_seta", "s_wgu2"):
        sems[s] = ctx.enter_context(nc.semaphore(s))

    sc = Sched()

    def walk():
        W = sc.wait
        O = sc.op
        DR = sc.drain

        def x_load(k, engine="gpsimd", sem=None):
            r0 = WARM + (k - 1) * T
            src = xe[r0:r0 + T, :].rearrange("(j p) c -> p j c", p=128)
            dst = xbuf[k % 2][:, :].rearrange("p (j c) -> p j c", c=D)
            O(engine, lambda eng, s=src, d=dst: eng.dma_start(d, s),
              inc=sem or f"s_x{k % 2}", n=16, mark=("x", k))

        def x0_half(half, sem):
            src = xe[half * 128:(half + 1) * 128, :]
            dst = xbuf[0][:, half * D:(half + 1) * D]
            O("sync", lambda eng, s=src, d=dst: eng.dma_start(d, s),
              inc=sem, n=16, mark=("x0h", half) if half == 0 else ("x", 0))

        x0_half(0, "s_xh0")
        for src, dst in ((eps_h, eps_sb), (ll_h, ll_sb), (ident_h, ident_sb)):
            O("sync", lambda eng, s=src, d=dst: eng.dma_start(d[:, :], s[:, :]),
              inc="s_seta", n=16)
        x0_half(1, "s_xh1")
        x_load(1, engine="sync", sem="s_xh2")
        for src, dst in ((gam_h, gam_sb), (bin_h, bin_sb), (bout_h, bout_sb),
                         (mask_h, mask_sb), (ones_h, ones_sb)):
            O("sync", lambda eng, s=src, d=dst: eng.dma_start(d[:, :], s[:, :]),
              inc="s_sets", n=16)
        for h in range(2):
            src_w = win_t[h * 512:(h + 1) * 512, :].rearrange(
                "(kc p) e -> p kc e", p=128)
            dst_w = winT[:, h * 4 * D:(h + 1) * 4 * D].rearrange(
                "p (kc e) -> p kc e", e=D)
            O("gpsimd", lambda eng, s=src_w, d=dst_w: eng.dma_start(d, s),
              inc="s_setg", n=16, mark=("winh", h))
            W("gpsimd", mark=("winh", h))
        src_w2 = wout_t.rearrange("(kc p) e -> p kc e", p=128)
        dst_w2 = woutT[:, :].rearrange("p (kc e) -> p kc e", e=D)
        O("gpsimd", lambda eng, s=src_w2, d=dst_w2: eng.dma_start(d, s),
          inc="s_setg", n=16, mark=("wcast", 1))



        W("tensor", "s_seta", value=NSETUP_A)


        def lnchain_a(k):
            p = k % 2
            if k == 0:
                W("vector", mark=("x0h", 0))
            else:
                W("vector", mark=("x", k))
            DR("vector")
            nj = 2 if k == 0 else 4
            for j in range(nj):
                if k == 0 and j == 1:
                    W("vector", mark=("x", 0))
                for h in range(2):
                    O("vector", lambda eng, j=j, h=h, p=p: eng.bn_stats(
                        stats[:, j * 12 + 6 * h: j * 12 + 6 * h + 6],
                        xbuf[p][:, j * D + 512 * h: j * D + 512 * h + 512]))
            DR("vector")
            for j in range(nj):
                O("vector", lambda eng, j=j: eng.bn_aggr(
                    mv[:, 2 * j:2 * j + 2], stats[:, j * 12:(j + 1) * 12]),
                    inc="s_v", mark=None if j < nj - 1 else ("aggr", k))

        def lnchain_b(k):
            nj = 2 if k == 0 else 4
            W("vector", mark=("sqrt", k))
            O("vector", lambda eng, nj=nj: eng.reciprocal(rstd4[:, 0:nj],
                                                          sq4[:, 0:nj]))
            DR("vector")
            O("vector", lambda eng, nj=nj: eng.scalar_tensor_tensor(
                negmr4[:, 0:nj], mv[:, 0:2 * nj:2], -1.0, rstd4[:, 0:nj],
                ALU.mult, ALU.mult),
                inc="s_v", mark=("lnchain", k))

        def lnchain(k):
            lnchain_a(k)
            lnchain_b(k)

        def scalar_sqrt(k):
            if k == 0:
                W("scalar", "s_seta", value=NSETUP_A)
            W("scalar", mark=("aggr", k))
            if k >= 1:
                W("scalar", mark=("lnchain", k - 1))
            DR("scalar")
            nj = 2 if k == 0 else 4
            O("scalar", lambda eng, nj=nj: eng.activation(
                sq4[:, 0:nj], mv[:, 1:2 * nj:2], AF.Sqrt, bias=eps_sb[:, 0:1]),
                inc="s_a", mark=("sqrt", k))

        def scalar_ln_apply(k):
            p = k % 2
            W("scalar", mark=("lnchain", k))
            W("scalar", mark=("x", k))
            if k == 0:
                W("scalar", mark=("x0h", 0))
            if k >= 2:
                W("scalar", mark=("sevac", k - 1, 7))
            elif k == 1:
                W("scalar", mark=("T", 0, 7))
            nj = 2 if k == 0 else 4
            for j in range(nj):
                O("scalar", lambda eng, j=j, p=p: eng.activation(
                    xnb[:, j * D:(j + 1) * D], xbuf[p][:, j * D:(j + 1) * D],
                    AF.Identity, bias=negmr4[:, j:j + 1], scale=rstd4[:, j:j + 1]),
                    inc="s_a", mark=None if j < nj - 1 else ("LN", k))
            O("scalar", lambda eng: eng.activation(dummy_sb[:, :], eps_sb[:, :],
                                                   AF.Silu))

        def tensor_tile(k):
            real = k >= 1
            W("tensor", mark=("LN", k))
            jw = 2 if k == 0 else 4
            for kc in range(NK):
                if kc >= 4:
                    W("tensor", mark=("Tevac", k, kc - 4))
                elif k >= 2:
                    W("tensor", mark=("dsilu", k - 1, kc))
                elif k == 1 and kc < 2:
                    W("tensor", mark=("guevac", 0, 6 + kc))
                for j in range(jw):
                    O("tensor", lambda eng, kc=kc, j=j: eng.transpose(
                        pb[kc % 4][:, j * 128:(j + 1) * 128],
                        xnb[:, j * D + kc * 128: j * D + kc * 128 + 128],
                        ident_sb[:, :]),
                        inc=None if j < jw - 1 else "s_t",
                        mark=None if j < jw - 1 else ("T", k, kc))
            for ec in range(NK):
                if ec == 0:
                    W("tensor", mark=("Tevac", k, 4))
                elif ec == 1:
                    W("tensor", mark=("Tevac", k, 5))
                else:
                    W("tensor", mark=("guevac", k, ec - 2))
                tw = WARM if k == 0 else T
                for kc in range(NK):
                    if ec == 0:
                        W("tensor", mark=("Tevac", k, kc))
                        if k == 0:
                            W("tensor", mark=("winh", kc // 4))
                    O("tensor", lambda eng, ec=ec, kc=kc, tw=tw: eng.matmul(
                        pb[ec % 2][:, 0:tw],
                        winT[:, kc * D + ec * 128: kc * D + ec * 128 + 128],
                        xnbT[:, kc * T: kc * T + tw],
                        start=(kc == 0), stop=(kc == NK - 1)),
                        inc=None if kc < NK - 1 else "s_t",
                        mark=None if kc < NK - 1 else ("U", k, ec))
            if not real:
                return
            for g in range(NGU):
                G = (k - 1) * NGU + g
                W("tensor", mark=("wgu", G))
                for cc in range(2):
                    c = 2 * g + cc
                    gb = 2 + 2 * (c % 3)
                    ub = 3 + 2 * (c % 3)
                    if c >= 3:
                        W("tensor", mark=("gsilu", k, c - 3))
                        W("tensor", mark=("gumul", k, c - 3))
                    elif k >= 2:
                        W("tensor", "s_a",
                          mark=("dsilu", k - 1, 3 + 2 * c))
                    for kc in range(NK):
                        O("tensor", lambda eng, g=g, cc=cc, kc=kc, gb=gb: eng.matmul(
                            pb[gb][:, 0:512],
                            wgubuf[g % 3][:, kc * 512 + cc * 128: kc * 512 + cc * 128 + 128],
                            xnbT[:, kc * T:(kc + 1) * T],
                            start=(kc == 0), stop=(kc == NK - 1)),
                            inc=None if kc < NK - 1 else "s_t",
                            mark=None if kc < NK - 1 else ("gmm", k, c))
                    for kc in range(NK):
                        O("tensor", lambda eng, g=g, cc=cc, kc=kc, ub=ub: eng.matmul(
                            pb[ub][:, 0:512],
                            wgubuf[g % 3][:, kc * 512 + 256 + cc * 128: kc * 512 + 256 + cc * 128 + 128],
                            xnbT[:, kc * T:(kc + 1) * T],
                            start=(kc == 0), stop=(kc == NK - 1)),
                            inc=None if kc < NK - 1 else "s_t",
                            mark=None if kc < NK - 1 else ("umm", k, c))
            if k == 1:
                W("tensor", mark=("wcast", 1))
                W("tensor", "s_sets", value=NSETUP_S)
            W("tensor", mark=("scanall", k))
            W("tensor", mark=("guevac", k, NK - 1))
            for b in range(8):
                m, n = b // 2, b % 2
                if b >= 2:
                    W("tensor", mark=("sevac", k, b - 2))
                O("tensor", lambda eng, b=b, n=n: eng.matmul(
                    pb[b % 2][:, 0:512], ones_sb[0:1, :],
                    bout_bf[0:1, n * 512:(n + 1) * 512],
                    start=True, stop=False))
                for kc in range(NK):
                    O("tensor", lambda eng, b=b, m=m, n=n, kc=kc: eng.matmul(
                        pb[b % 2][:, 0:512],
                        hTb[:, kc * T + m * 128: kc * T + m * 128 + 128],
                        woutT[:, kc * D + n * 512: kc * D + n * 512 + 512],
                        start=False, stop=(kc == NK - 1)),
                        inc=None if kc < NK - 1 else "s_t",
                        mark=None if kc < NK - 1 else ("S", k, b))
            W("tensor", mark=("gumul", k, NHC - 1))
            W("tensor", mark=("gsilu", k, NHC - 1))
            W("tensor", mark=("wdload", k))
            for m in (1, 0, 2, 3):
                if m == 0:
                    W("tensor", mark=("sevac", k, 7))
                if k >= 2:
                    W("tensor", mark=("dsilu", k - 1, 2 * m + 1))
                for c in range(NHC):
                    for n in range(2):
                        last = (c == NHC - 1 and n == 1)
                        O("tensor", lambda eng, c=c, m=m, n=n: eng.matmul(
                            pb[2 * m + n][:, 0:512],
                            hbT[:, c * T + m * 128: c * T + m * 128 + 128],
                            wdfull[:, c * D + n * 512: c * D + n * 512 + 512],
                            start=(c == 0), stop=(c == NHC - 1)),
                            inc="s_t" if last else None,
                            mark=("down_m", k, m) if last else None)

        def dsilu_emit(k, bs):
            for b in bs:
                W("scalar", mark=("down_m", k, b // 2))
                if b >= 4:
                    W("scalar", mark=("comb", k, b - 4))
                elif k >= 2:
                    W("scalar", mark=("comb", k - 1, 4 + b))
                O("scalar", lambda eng, b=b: eng.activation(
                    gslab[b % 4][:, :], pb[b][:, 0:512], AF.Silu, scale=QINV),
                    inc="s_a", mark=("dsilu", k, b))

        def scalar_tile(k):
            real = k >= 1
            if real:
                if k >= 2:
                    W("scalar", mark=("umm", k - 1, NHC - 1))
                else:
                    W("scalar", mark=("U", 0, NK - 1))
            tw = WARM if k == 0 else T
            for kc in range(0, NK, 2):
                W("scalar", mark=("T", k, kc))
                O("scalar", lambda eng, kc=kc, tw=tw: eng.copy(
                    xnbT[:, kc * T: kc * T + tw], pb[kc % 4][:, 0:tw]),
                    inc="s_a", mark=("Tevac", k, kc))
            if k >= 2:
                dsilu_emit(k - 1, (4, 5, 6, 7))
            if k == 0:
                scalar_sqrt(1)
            if k == 0:
                W("scalar", mark=("setupV",))
            if k >= 1:
                W("scalar", mark=("carry", k - 1))
            if k >= 2:
                W("scalar", mark=("S", k - 1, 7))
            for ec in range(NK):
                W("scalar", mark=("U", k, ec))
                O("scalar", lambda eng, ec=ec, tw=tw: eng.activation(
                    guT[:, ec * T: ec * T + tw], pb[ec % 2][:, 0:tw],
                    AF.Identity, bias=bgam_sb[:, ec:ec + 1],
                    scale=gam_sb[:, ec:ec + 1]),
                    inc="s_a", mark=("guevac", k, ec))
            if not real:
                scalar_ln_apply(1)
                return
            for c in range(NHC):
                W("scalar", mark=("gmm", k, c))
                if c >= 2:
                    W("scalar", mark=("gumul", k, c - 2))
                elif k >= 2:
                    W("scalar", mark=("comb", k - 1, 4 + c))
                O("scalar", lambda eng, c=c: eng.activation(
                    siluT[c % 2][:, :], pb[2 + 2 * (c % 3)][:, 0:512],
                    AF.Silu, scale=QINV),
                    inc="s_a", mark=("gsilu", k, c))
            if k < NT:
                scalar_sqrt(k + 1)
                scalar_ln_apply(k + 1)
            dsilu_emit(k, (2, 3, 0, 1))
            if k == NT:
                dsilu_emit(k, (4, 5, 6, 7))

        def vector_tile(k):
            real = k >= 1
            if k == 0:
                lnchain(0)
                W("vector", "s_sets", value=NSETUP_S)
                O("vector", lambda eng: eng.tensor_mul(
                    bgam_sb[:, :], bin_sb[:, :], gam_sb[:, :]),
                    inc="s_v", mark=("setupV",))
                DR("vector")
                lnchain_a(1)
            if k >= 2:
                W("vector", mark=("umm", k - 1, NHC - 1))
            elif k == 1:
                W("vector", mark=("U", 0, NK - 1))
            tw = WARM if k == 0 else T
            for kc in range(1, NK, 2):
                W("vector", mark=("T", k, kc))
                O("vector", lambda eng, kc=kc, tw=tw: eng.tensor_copy(
                    xnbT[:, kc * T: kc * T + tw], pb[kc % 4][:, 0:tw]),
                    inc="s_v", mark=("Tevac", k, kc))
            def scan_emit(ec):
                W("vector", mark=("guevac", k, ec))
                init = 0.0 if k == 0 else carry[:, ec:ec + 1]
                O("vector", lambda eng, ec=ec, init=init, tw=tw:
                    eng.tensor_tensor_scan(
                        hTb[:, ec * T: ec * T + tw],
                        lam_sb[:, ec:ec + 1].broadcast_to([128, tw]),
                        guT[:, ec * T: ec * T + tw],
                        init, ALU.mult, ALU.add),
                    inc="s_v", mark=None if ec < NK - 1 else ("scanall", k))

            def carry_emit():
                DR("vector")
                for ec in range(NK):
                    O("vector", lambda eng, ec=ec, tw=tw: eng.tensor_copy(
                        carry[:, ec:ec + 1],
                        hTb[:, ec * T + tw - 1: ec * T + tw]),
                        inc=None if ec < NK - 1 else "s_v",
                        mark=None if ec < NK - 1 else ("carry", k))

            DR("vector")
            if k >= 2:
                W("vector", mark=("S", k - 1, 7))
            if k == 0:
                for ec in range(NK):
                    scan_emit(ec)
                carry_emit()
                DR("vector")
                O("vector", lambda eng: eng.tensor_scalar_mul(
                    carry[:, :], carry[:, :], mask_sb[:, 0:1]))
                DR("vector")
                lnchain_b(1)
                return
            if k >= 2:
                W("vector", mark=("down_m", k - 1, 3))
            for c in range(NHC):
                W("vector", mark=("umm", k, c))
                W("vector", mark=("gsilu", k, c))
                O("vector", lambda eng, c=c: eng.scalar_tensor_tensor(
                    hbT[:, c * T:(c + 1) * T], pb[3 + 2 * (c % 3)][:, 0:512],
                    QINV, siluT[c % 2][:, :], ALU.mult, ALU.mult),
                    inc="s_v", mark=("gumul", k, c))
                if c % 3 == 1:
                    scan_emit(c // 3)
            carry_emit()
            DR("vector")
            if k >= 2:
                W("vector", mark=("comb", k - 1, 7))
            for b in range(8):
                m, n = b // 2, b % 2
                W("vector", mark=("S", k, b))
                O("vector", lambda eng, b=b, m=m, n=n: eng.tensor_tensor(
                    s_sb[:, m * D + n * 512: m * D + n * 512 + 512],
                    pb[b % 2][:, 0:512],
                    xnb[:, m * D + n * 512: m * D + n * 512 + 512], ALU.add),
                    inc="s_v", mark=("sevac", k, b))
            if k < NT:
                lnchain(k + 1)
            else:
                for b in (3, 1, 5, 7):
                    m, n = b // 2, b % 2
                    W("vector", mark=("dsilu", k, b))
                    O("vector", lambda eng, b=b, m=m, n=n: eng.tensor_mul(
                        gslab[b % 4][:, :], gslab[b % 4][:, :],
                        s_sb[:, m * D + n * 512: m * D + n * 512 + 512]))
                    DR("vector")
                    O("vector", lambda eng, b=b, m=m, n=n: eng.tensor_add(
                        xbuf[k % 2][:, m * D + n * 512: m * D + n * 512 + 512],
                        xbuf[k % 2][:, m * D + n * 512: m * D + n * 512 + 512],
                        gslab[b % 4][:, :]),
                        inc="s_v", mark=("comb", k, b))
                    DR("vector")

        def sync_tile(k):
            if k == 0:
                return
            for g in range(NGU):
                G = (k - 1) * NGU + g
                if G >= 3:
                    k2, g2 = (G - 3) // NGU + 1, (G - 3) % NGU
                    W("sync", mark=("umm", k2, 2 * g2 + 1))
                src = wgu_t[:, g * 512:(g + 1) * 512].rearrange(
                    "(kc p) h -> p kc h", p=128)
                dst = wgubuf[G % 3][:, :].rearrange("p (kc h) -> p kc h", h=512)
                O("sync", lambda eng, s=src, d=dst: eng.dma_start(
                    d, s), inc=("s_wgu0", "s_wgu1", "s_wgu2")[G % 3], n=16,
                    mark=("wgu", G))

        def wd_load(k):
            for g in range(4):
                src = wd_t[g * 768:(g + 1) * 768, :].rearrange(
                    "(c p) d2 -> p c d2", p=128)
                dst = wdfull[:, g * 6 * D:(g + 1) * 6 * D].rearrange(
                    "p (c d2) -> p c d2", d2=D)
                O("gpsimd", lambda eng, s=src, d=dst: eng.dma_start(d, s),
                  inc="s_wdf", n=16, mark=None if g < 3 else ("wdload", k))

        def gpsimd_tile(k):
            if k == 0:
                wd_load(1)
                W("gpsimd", mark=("LN", 0))
                W("gpsimd", mark=("lnchain", 0))
                x_load(2)
                return
            p = k % 2
            for b in ((2, 3, 0, 1, 4, 5, 6, 7) if k < NT else (2, 0, 4, 6)):
                m, n = b // 2, b % 2
                W("gpsimd", mark=("dsilu", k, b))
                O("gpsimd", lambda eng, b=b, m=m, n=n: eng.tensor_mul(
                    gslab[b % 4][:, :], gslab[b % 4][:, :],
                    s_sb[:, m * D + n * 512: m * D + n * 512 + 512]))
                DR("gpsimd")
                O("gpsimd", lambda eng, b=b, m=m, n=n, p=p: eng.tensor_add(
                    xbuf[p][:, m * D + n * 512: m * D + n * 512 + 512],
                    xbuf[p][:, m * D + n * 512: m * D + n * 512 + 512],
                    gslab[b % 4][:, :]),
                    inc="s_g", mark=("comb", k, b))
                DR("gpsimd")
                if b % 2 == 1 or k == NT:
                    if k == NT:
                        W("gpsimd", mark=("comb", k, b + 1))
                    src = xbuf[p][:, m * D:(m + 1) * D]
                    dst = out_h[(k - 1) * T + m * 128: (k - 1) * T + m * 128 + 128, :]
                    O("gpsimd", lambda eng, s=src, d=dst: eng.dma_start(d, s),
                      inc=f"s_out{k % 2}", n=16,
                      mark=("store", k) if m == 3 else None)
                    if m == 0 and k < NT:
                        W("gpsimd", mark=("down_m", k, 3))
                        wd_load(k + 1)
            if k + 2 <= NT:
                W("gpsimd", mark=("store", k))
                x_load(k + 2)

        scalar_sqrt(0)
        scalar_ln_apply(0)
        W("scalar", "s_seta", value=NSETUP_A)
        O("scalar", lambda eng: eng.activation(lam_sb[:, :], ll_sb[:, :], AF.Exp))
        W("scalar", "s_sets", value=NSETUP_S)
        O("scalar", lambda eng: eng.copy(bout_bf[:, :], bout_sb[:, :]),
          inc="s_a", mark=("setupA",))
        DR("scalar")

        for k in range(NT + 1):
            tensor_tile(k)
            scalar_tile(k)
            vector_tile(k)
            sync_tile(k)
            gpsimd_tile(k)
        W("gpsimd", mark=("store", NT))

    sc.recording = True
    walk()
    sc.recording = False
    sc.cnt.clear()
    walk()

    def play(eng, name):
        for item in sc.ops[name]:
            if item[0] == "op":
                _, make, inc, n = item
                ins = make(eng)
                if inc is not None:
                    ins.then_inc(sems[inc], n)
            else:
                _, sem, v = item
                eng.wait_ge(sems[sem], v)

    with nc.Block() as block:
        @block.tensor
        def _(eng):
            play(eng, "tensor")

        @block.vector
        def _(eng):
            play(eng, "vector")

        @block.scalar
        def _(eng):
            play(eng, "scalar")

        @block.sync
        def _(eng):
            play(eng, "sync")

        @block.gpsimd
        def _(eng):
            play(eng, "gpsimd")

    ctx.close()
    return nc


def make_in_maps(inputs):
    x = np.ascontiguousarray(np.asarray(inputs["L0_attn_x"], dtype=np.float32))
    lam_log = np.asarray(inputs["L0_attn_lambda_log"], dtype=np.float32)
    gamma = np.asarray(inputs["L0_attn_gamma"], dtype=np.float32)
    w_in = np.asarray(inputs["L0_attn_W_in"], dtype=np.float32)
    b_in = np.asarray(inputs["L0_attn_b_in"], dtype=np.float32)
    w_out = np.asarray(inputs["L0_attn_W_out"], dtype=np.float32)
    b_out = np.asarray(inputs["L0_attn_b_out"], dtype=np.float32)
    wg = np.asarray(inputs["L0_ffn_w_gate"])
    wu = np.asarray(inputs["L0_ffn_w_up"])
    wd = np.asarray(inputs["L0_ffn_w_down"])

    bf = ml_dtypes.bfloat16
    shared = {
        "win_t": np.ascontiguousarray(w_in.T),
        "wout_t": np.ascontiguousarray(w_out.T),
        "wgu_t": np.ascontiguousarray(np.concatenate(
            [np.ascontiguousarray(wg.T).reshape(D, NGU, 256),
             np.ascontiguousarray(wu.T).reshape(D, NGU, 256)],
            axis=2).reshape(D, 2 * H)).astype(bf),
        "wd_t": np.ascontiguousarray(wd.T).astype(bf),
        "ll": np.ascontiguousarray(lam_log.reshape(NK, 128).T),
        "gam": np.ascontiguousarray(gamma.reshape(NK, 128).T),
        "bin": np.ascontiguousarray(b_in.reshape(NK, 128).T),
        "bout": np.ascontiguousarray(b_out.reshape(1, D)),
        "ident": np.eye(128, dtype=np.float32),
        "ones1": np.ones((1, 128), dtype=bf),
        "epsv": np.full((128, 1), 1e-5, dtype=np.float32),
    }
    in_maps = []
    for core in range(8):
        b, q = core // 4, core % 4
        own = x[b, q * 4096:(q + 1) * 4096, :]
        if q > 0:
            warm = x[b, q * 4096 - WARM: q * 4096, :]
            mask = np.ones((128, 1), dtype=np.float32)
        else:
            warm = x[b, 0:WARM, :]
            mask = np.zeros((128, 1), dtype=np.float32)
        m = dict(shared)
        m["xe"] = np.ascontiguousarray(np.concatenate([warm, own], axis=0))
        m["mask"] = mask
        in_maps.append(m)
    return in_maps


def kernel(**inputs):
    nc = build_nc()
    in_maps = make_in_maps(inputs)
    res = run_bass_kernel_spmd(nc, in_maps, core_ids=list(range(8)))
    out = np.empty((2, 16384, D), dtype=np.float32)
    for core in range(8):
        b, q = core // 4, core % 4
        out[b, q * 4096:(q + 1) * 4096, :] = res.results[core]["out"]
    return out


# version 26 (final)
# speedup vs baseline: 1.0027x; 1.0027x over previous
import contextlib
from collections import defaultdict

import numpy as np
import ml_dtypes

import concourse.bass as bass
import concourse.mybir as mybir
from concourse.bass_utils import run_bass_kernel_spmd

F32 = mybir.dt.float32
BF16 = mybir.dt.bfloat16
AF = mybir.ActivationFunctionType
ALU = mybir.AluOpType

D = 1024
H = 3072
T = 512
NT = 8
NK = D // 128
NHC = H // 128
NGU = H // 256
WARM = 256
QINV = 1.0 / 127.0
NSETUP_G = 16 * 2
NSETUP_A = 16 * 3
NSETUP_S = 16 * 5


class Sched:

    def __init__(self):
        self.recording = True
        self.cnt = defaultdict(int)
        self.marks = {}
        self.ops = defaultdict(list)

    def op(self, engine, make, inc=None, n=1, mark=None):
        if self.recording:
            if inc is not None:
                self.cnt[inc] += n
            if mark is not None:
                assert mark not in self.marks, f"dup mark {mark}"
                assert inc is not None, f"mark {mark} needs inc"
                self.marks[mark] = (inc, self.cnt[inc])
        else:
            self.ops[engine].append(("op", make, inc, n))

    def wait(self, engine, sem=None, mark=None, value=None):
        if self.recording:
            return
        if mark is not None:
            msem, v = self.marks[mark]
            assert sem is None or sem == msem, f"sem mismatch at {mark}"
            sem = msem
        else:
            v = value
        self.ops[engine].append(("wait", sem, v))

    def drain(self, engine):
        self.op(engine, lambda eng: eng.drain())


def build_nc():
    nc = bass.Bass("TRN2", target_bir_lowering=False)

    di = lambda name, sh, dt=F32: nc.dram_tensor(name, sh, dt, kind="ExternalInput").ap()
    xe = di("xe", [NT * T + WARM, D])
    win_t = di("win_t", [D, D])
    wout_t = di("wout_t", [D, D])
    wgu_t = di("wgu_t", [D, 2 * H], BF16)
    wd_t = di("wd_t", [H, D], BF16)
    ll_h = di("ll", [128, NK])
    gam_h = di("gam", [128, NK])
    bin_h = di("bin", [128, NK])
    bout_h = di("bout", [1, D])
    mask_h = di("mask", [128, 1])
    ident_h = di("ident", [128, 128])
    ones_h = di("ones1", [1, 128], BF16)
    out_h = nc.dram_tensor("out", [NT * T, D], F32, kind="ExternalOutput").ap()

    ctx = contextlib.ExitStack()
    sb = lambda name, sh, dt: ctx.enter_context(nc.sbuf_tensor(name, sh, dt))

    winT = sb("winT", [128, NK * D], BF16)
    woutT = sb("woutT", [128, NK * D], BF16)
    wgubuf = [sb(f"wgubuf{i}", [128, NK * 512], BF16) for i in range(3)]
    wdfull = sb("wdfull", [128, NHC * D], BF16)
    xbuf = [sb(f"xbuf{i}", [128, 4 * D], F32) for i in range(2)]
    xnb = sb("xnb", [128, 4 * D], F32)
    xnbT = sb("xnbT", [128, NK * T], BF16)
    guT = sb("guT", [128, NK * T], BF16)
    hTb = guT
    carry = sb("carry", [128, NK], F32)
    s_sb = sb("s_sb", [128, 4 * D], BF16)
    hbT = sb("hbT", [128, NHC * T], BF16)
    gslab = [sb(f"gslab{i}", [128, T], BF16) for i in range(4)]
    siluT = gslab[:2]
    stats = sb("stats", [128, 48], F32)
    mv = sb("mv", [128, 8], F32)
    sq4 = sb("sq4", [128, 4], F32)
    rstd4 = sb("rstd4", [128, 4], F32)
    negmr4 = sb("negmr4", [128, 4], F32)
    lam_sb = sb("lam_sb", [128, NK], F32)
    ll_sb = sb("ll_sb", [128, NK], F32)
    gam_sb = sb("gam_sb", [128, NK], F32)
    bin_sb = sb("bin_sb", [128, NK], F32)
    bgam_sb = sb("bgam_sb", [128, NK], F32)
    bout_bf = sb("bout_bf", [1, D], BF16)
    mask_sb = sb("mask_sb", [128, 1], F32)
    ident_sb = sb("ident_sb", [128, 128], F32)
    ones_sb = sb("ones_sb", [1, 128], BF16)
    bout_sb = xnb[0:1, 3 * D:4 * D]
    eps_h = di_eps = nc.dram_tensor("epsv", [128, 1], F32,
                                    kind="ExternalInput").ap()
    eps_sb = sb("eps_sb", [128, 1], F32)
    dummy_sb = sb("dummy_sb", [128, 1], F32)

    pb = [ctx.enter_context(nc.psum_tensor(f"pb{i}", [128, 512], F32))
          for i in range(8)]

    sems = {}
    for s in ("s_x0", "s_x1", "s_out0", "s_out1", "s_wgu0", "s_wgu1",
              "s_wdf", "s_setg", "s_sets", "s_t", "s_v", "s_a", "s_g",
              "s_xh0", "s_xh1", "s_xh2", "s_seta", "s_wgu2"):
        sems[s] = ctx.enter_context(nc.semaphore(s))

    sc = Sched()

    def walk():
        W = sc.wait
        O = sc.op
        DR = sc.drain

        def x_load(k, engine="gpsimd", sem=None):
            r0 = WARM + (k - 1) * T
            src = xe[r0:r0 + T, :].rearrange("(j p) c -> p j c", p=128)
            dst = xbuf[k % 2][:, :].rearrange("p (j c) -> p j c", c=D)
            O(engine, lambda eng, s=src, d=dst: eng.dma_start(d, s),
              inc=sem or f"s_x{k % 2}", n=16, mark=("x", k))

        def x0_half(half, sem):
            src = xe[half * 128:(half + 1) * 128, :]
            dst = xbuf[0][:, half * D:(half + 1) * D]
            O("sync", lambda eng, s=src, d=dst: eng.dma_start(d, s),
              inc=sem, n=16, mark=("x0h", half) if half == 0 else ("x", 0))

        x0_half(0, "s_xh0")
        for src, dst in ((eps_h, eps_sb), (ll_h, ll_sb), (ident_h, ident_sb)):
            O("sync", lambda eng, s=src, d=dst: eng.dma_start(d[:, :], s[:, :]),
              inc="s_seta", n=16)
        x0_half(1, "s_xh1")
        x_load(1, engine="sync", sem="s_xh2")
        for src, dst in ((gam_h, gam_sb), (bin_h, bin_sb), (bout_h, bout_sb),
                         (mask_h, mask_sb), (ones_h, ones_sb)):
            O("sync", lambda eng, s=src, d=dst: eng.dma_start(d[:, :], s[:, :]),
              inc="s_sets", n=16)
        for h in range(2):
            src_w = win_t[h * 512:(h + 1) * 512, :].rearrange(
                "(kc p) e -> p kc e", p=128)
            dst_w = winT[:, h * 4 * D:(h + 1) * 4 * D].rearrange(
                "p (kc e) -> p kc e", e=D)
            O("gpsimd", lambda eng, s=src_w, d=dst_w: eng.dma_start(d, s),
              inc="s_setg", n=16, mark=("winh", h))
            W("gpsimd", mark=("winh", h))
        src_w2 = wout_t.rearrange("(kc p) e -> p kc e", p=128)
        dst_w2 = woutT[:, :].rearrange("p (kc e) -> p kc e", e=D)
        O("gpsimd", lambda eng, s=src_w2, d=dst_w2: eng.dma_start(d, s),
          inc="s_setg", n=16, mark=("wcast", 1))



        W("tensor", "s_seta", value=NSETUP_A)


        def lnchain_a(k):
            p = k % 2
            if k == 0:
                W("vector", mark=("x0h", 0))
            else:
                W("vector", mark=("x", k))
            DR("vector")
            nj = 2 if k == 0 else 4
            for j in range(nj):
                if k == 0 and j == 1:
                    W("vector", mark=("x", 0))
                for h in range(2):
                    O("vector", lambda eng, j=j, h=h, p=p: eng.bn_stats(
                        stats[:, j * 12 + 6 * h: j * 12 + 6 * h + 6],
                        xbuf[p][:, j * D + 512 * h: j * D + 512 * h + 512]))
            DR("vector")
            for j in range(nj):
                O("vector", lambda eng, j=j: eng.bn_aggr(
                    mv[:, 2 * j:2 * j + 2], stats[:, j * 12:(j + 1) * 12]),
                    inc="s_v", mark=None if j < nj - 1 else ("aggr", k))

        def lnchain_b(k):
            nj = 2 if k == 0 else 4
            W("vector", mark=("sqrt", k))
            O("vector", lambda eng, nj=nj: eng.reciprocal(rstd4[:, 0:nj],
                                                          sq4[:, 0:nj]))
            DR("vector")
            O("vector", lambda eng, nj=nj: eng.scalar_tensor_tensor(
                negmr4[:, 0:nj], mv[:, 0:2 * nj:2], -1.0, rstd4[:, 0:nj],
                ALU.mult, ALU.mult),
                inc="s_v", mark=("lnchain", k))

        def lnchain(k):
            lnchain_a(k)
            lnchain_b(k)

        def scalar_sqrt(k):
            if k == 0:
                W("scalar", "s_seta", value=NSETUP_A)
            W("scalar", mark=("aggr", k))
            if k >= 1:
                W("scalar", mark=("lnchain", k - 1))
            DR("scalar")
            nj = 2 if k == 0 else 4
            O("scalar", lambda eng, nj=nj: eng.activation(
                sq4[:, 0:nj], mv[:, 1:2 * nj:2], AF.Sqrt, bias=eps_sb[:, 0:1]),
                inc="s_a", mark=("sqrt", k))

        def scalar_ln_apply(k):
            p = k % 2
            W("scalar", mark=("lnchain", k))
            W("scalar", mark=("x", k))
            if k == 0:
                W("scalar", mark=("x0h", 0))
            if k >= 2:
                W("scalar", mark=("sevac", k - 1, 7))
            elif k == 1:
                W("scalar", mark=("T", 0, 7))
            nj = 2 if k == 0 else 4
            for j in range(nj):
                O("scalar", lambda eng, j=j, p=p: eng.activation(
                    xnb[:, j * D:(j + 1) * D], xbuf[p][:, j * D:(j + 1) * D],
                    AF.Identity, bias=negmr4[:, j:j + 1], scale=rstd4[:, j:j + 1]),
                    inc="s_a", mark=None if j < nj - 1 else ("LN", k))
            O("scalar", lambda eng: eng.activation(dummy_sb[:, :], eps_sb[:, :],
                                                   AF.Silu))

        def tensor_tile(k):
            real = k >= 1
            W("tensor", mark=("LN", k))
            jw = 2 if k == 0 else 4
            for kc in range(NK):
                if kc >= 4:
                    W("tensor", mark=("Tevac", k, kc - 4))
                elif k >= 2:
                    W("tensor", mark=("dsilu", k - 1, kc))
                elif k == 1 and kc < 2:
                    W("tensor", mark=("guevac", 0, 6 + kc))
                for j in range(jw):
                    O("tensor", lambda eng, kc=kc, j=j: eng.transpose(
                        pb[kc % 4][:, j * 128:(j + 1) * 128],
                        xnb[:, j * D + kc * 128: j * D + kc * 128 + 128],
                        ident_sb[:, :]),
                        inc=None if j < jw - 1 else "s_t",
                        mark=None if j < jw - 1 else ("T", k, kc))
            for ec in range(NK):
                if ec == 0:
                    W("tensor", mark=("Tevac", k, 4))
                elif ec == 1:
                    W("tensor", mark=("Tevac", k, 5))
                else:
                    W("tensor", mark=("guevac", k, ec - 2))
                tw = WARM if k == 0 else T
                for kc in range(NK):
                    if ec == 0:
                        W("tensor", mark=("Tevac", k, kc))
                        if k == 0:
                            W("tensor", mark=("winh", kc // 4))
                    O("tensor", lambda eng, ec=ec, kc=kc, tw=tw: eng.matmul(
                        pb[ec % 2][:, 0:tw],
                        winT[:, kc * D + ec * 128: kc * D + ec * 128 + 128],
                        xnbT[:, kc * T: kc * T + tw],
                        start=(kc == 0), stop=(kc == NK - 1)),
                        inc=None if kc < NK - 1 else "s_t",
                        mark=None if kc < NK - 1 else ("U", k, ec))
            if not real:
                return
            for g in range(NGU):
                G = (k - 1) * NGU + g
                W("tensor", mark=("wgu", G))
                for cc in range(2):
                    c = 2 * g + cc
                    gb = 2 + 2 * (c % 3)
                    ub = 3 + 2 * (c % 3)
                    if c >= 3:
                        W("tensor", mark=("gsilu", k, c - 3))
                        W("tensor", mark=("gumul", k, c - 3))
                    elif k >= 2:
                        W("tensor", "s_a",
                          mark=("dsilu", k - 1, 3 + 2 * c))
                    for kc in range(NK):
                        O("tensor", lambda eng, g=g, cc=cc, kc=kc, gb=gb: eng.matmul(
                            pb[gb][:, 0:512],
                            wgubuf[g % 3][:, kc * 512 + cc * 128: kc * 512 + cc * 128 + 128],
                            xnbT[:, kc * T:(kc + 1) * T],
                            start=(kc == 0), stop=(kc == NK - 1)),
                            inc=None if kc < NK - 1 else "s_t",
                            mark=None if kc < NK - 1 else ("gmm", k, c))
                    for kc in range(NK):
                        O("tensor", lambda eng, g=g, cc=cc, kc=kc, ub=ub: eng.matmul(
                            pb[ub][:, 0:512],
                            wgubuf[g % 3][:, kc * 512 + 256 + cc * 128: kc * 512 + 256 + cc * 128 + 128],
                            xnbT[:, kc * T:(kc + 1) * T],
                            start=(kc == 0), stop=(kc == NK - 1)),
                            inc=None if kc < NK - 1 else "s_t",
                            mark=None if kc < NK - 1 else ("umm", k, c))
            if k == 1:
                W("tensor", mark=("wcast", 1))
                W("tensor", "s_sets", value=NSETUP_S)
            W("tensor", mark=("scanall", k))
            W("tensor", mark=("guevac", k, NK - 1))
            for b in range(8):
                m, n = b // 2, b % 2
                if b >= 2:
                    W("tensor", mark=("sevac", k, b - 2))
                O("tensor", lambda eng, b=b, n=n: eng.matmul(
                    pb[b % 2][:, 0:512], ones_sb[0:1, :],
                    bout_bf[0:1, n * 512:(n + 1) * 512],
                    start=True, stop=False))
                for kc in range(NK):
                    O("tensor", lambda eng, b=b, m=m, n=n, kc=kc: eng.matmul(
                        pb[b % 2][:, 0:512],
                        hTb[:, kc * T + m * 128: kc * T + m * 128 + 128],
                        woutT[:, kc * D + n * 512: kc * D + n * 512 + 512],
                        start=False, stop=(kc == NK - 1)),
                        inc=None if kc < NK - 1 else "s_t",
                        mark=None if kc < NK - 1 else ("S", k, b))
            W("tensor", mark=("gumul", k, NHC - 1))
            W("tensor", mark=("gsilu", k, NHC - 1))
            W("tensor", mark=("wdload", k))
            for m in (1, 0, 2, 3):
                if m == 0:
                    W("tensor", mark=("sevac", k, 7))
                if k >= 2:
                    W("tensor", mark=("dsilu", k - 1, 2 * m + 1))
                for c in range(NHC):
                    for n in range(2):
                        last = (c == NHC - 1 and n == 1)
                        O("tensor", lambda eng, c=c, m=m, n=n: eng.matmul(
                            pb[2 * m + n][:, 0:512],
                            hbT[:, c * T + m * 128: c * T + m * 128 + 128],
                            wdfull[:, c * D + n * 512: c * D + n * 512 + 512],
                            start=(c == 0), stop=(c == NHC - 1)),
                            inc="s_t" if last else None,
                            mark=("down_m", k, m) if last else None)

        def dsilu_emit(k, bs):
            for b in bs:
                W("scalar", mark=("down_m", k, b // 2))
                if b >= 4:
                    W("scalar", mark=("comb", k, b - 4))
                elif k >= 2:
                    W("scalar", mark=("comb", k - 1, 4 + b))
                O("scalar", lambda eng, b=b: eng.activation(
                    gslab[b % 4][:, :], pb[b][:, 0:512], AF.Silu, scale=QINV),
                    inc="s_a", mark=("dsilu", k, b))

        def scalar_tile(k):
            real = k >= 1
            if real:
                if k >= 2:
                    W("scalar", mark=("umm", k - 1, NHC - 1))
                else:
                    W("scalar", mark=("U", 0, NK - 1))
            tw = WARM if k == 0 else T
            for kc in range(0, NK, 2):
                W("scalar", mark=("T", k, kc))
                O("scalar", lambda eng, kc=kc, tw=tw: eng.copy(
                    xnbT[:, kc * T: kc * T + tw], pb[kc % 4][:, 0:tw]),
                    inc="s_a", mark=("Tevac", k, kc))
            if k >= 2:
                dsilu_emit(k - 1, (4, 5, 6, 7))
            if k == 0:
                scalar_sqrt(1)
            if k == 0:
                W("scalar", mark=("setupV",))
            if k >= 1:
                W("scalar", mark=("carry", k - 1))
            if k >= 2:
                W("scalar", mark=("S", k - 1, 7))
            for ec in range(NK):
                W("scalar", mark=("U", k, ec))
                O("scalar", lambda eng, ec=ec, tw=tw: eng.activation(
                    guT[:, ec * T: ec * T + tw], pb[ec % 2][:, 0:tw],
                    AF.Identity, bias=bgam_sb[:, ec:ec + 1],
                    scale=gam_sb[:, ec:ec + 1]),
                    inc="s_a", mark=("guevac", k, ec))
            if not real:
                scalar_ln_apply(1)
                return
            for c in range(NHC):
                W("scalar", mark=("gmm", k, c))
                if c >= 2:
                    W("scalar", mark=("gumul", k, c - 2))
                elif k >= 2:
                    W("scalar", mark=("comb", k - 1, 4 + c))
                O("scalar", lambda eng, c=c: eng.activation(
                    siluT[c % 2][:, :], pb[2 + 2 * (c % 3)][:, 0:512],
                    AF.Silu, scale=QINV),
                    inc="s_a", mark=("gsilu", k, c))
            if k < NT:
                scalar_sqrt(k + 1)
                scalar_ln_apply(k + 1)
            dsilu_emit(k, (2, 3, 0, 1))
            if k == NT:
                dsilu_emit(k, (4, 5, 6, 7))

        def vector_tile(k):
            real = k >= 1
            if k == 0:
                lnchain(0)
                W("vector", "s_sets", value=NSETUP_S)
                O("vector", lambda eng: eng.tensor_mul(
                    bgam_sb[:, :], bin_sb[:, :], gam_sb[:, :]),
                    inc="s_v", mark=("setupV",))
                DR("vector")
                lnchain_a(1)
            if k >= 2:
                W("vector", mark=("umm", k - 1, NHC - 1))
            elif k == 1:
                W("vector", mark=("U", 0, NK - 1))
            tw = WARM if k == 0 else T
            for kc in range(1, NK, 2):
                W("vector", mark=("T", k, kc))
                O("vector", lambda eng, kc=kc, tw=tw: eng.tensor_copy(
                    xnbT[:, kc * T: kc * T + tw], pb[kc % 4][:, 0:tw]),
                    inc="s_v", mark=("Tevac", k, kc))
            def scan_emit(ec):
                W("vector", mark=("guevac", k, ec))
                init = 0.0 if k == 0 else carry[:, ec:ec + 1]
                O("vector", lambda eng, ec=ec, init=init, tw=tw:
                    eng.tensor_tensor_scan(
                        hTb[:, ec * T: ec * T + tw],
                        lam_sb[:, ec:ec + 1].broadcast_to([128, tw]),
                        guT[:, ec * T: ec * T + tw],
                        init, ALU.mult, ALU.add),
                    inc="s_v", mark=None if ec < NK - 1 else ("scanall", k))

            def carry_emit():
                DR("vector")
                for ec in range(NK):
                    O("vector", lambda eng, ec=ec, tw=tw: eng.tensor_copy(
                        carry[:, ec:ec + 1],
                        hTb[:, ec * T + tw - 1: ec * T + tw]),
                        inc=None if ec < NK - 1 else "s_v",
                        mark=None if ec < NK - 1 else ("carry", k))

            DR("vector")
            if k >= 2:
                W("vector", mark=("S", k - 1, 7))
            if k == 0:
                lnchain_b(1)
                for ec in range(NK):
                    scan_emit(ec)
                carry_emit()
                DR("vector")
                O("vector", lambda eng: eng.tensor_scalar_mul(
                    carry[:, :], carry[:, :], mask_sb[:, 0:1]))
                DR("vector")
                return
            if k >= 2:
                W("vector", mark=("down_m", k - 1, 3))
            for c in range(NHC):
                W("vector", mark=("umm", k, c))
                W("vector", mark=("gsilu", k, c))
                O("vector", lambda eng, c=c: eng.scalar_tensor_tensor(
                    hbT[:, c * T:(c + 1) * T], pb[3 + 2 * (c % 3)][:, 0:512],
                    QINV, siluT[c % 2][:, :], ALU.mult, ALU.mult),
                    inc="s_v", mark=("gumul", k, c))
                if c % 3 == 1:
                    scan_emit(c // 3)
            carry_emit()
            DR("vector")
            if k >= 2:
                W("vector", mark=("comb", k - 1, 7))
            for b in range(8):
                m, n = b // 2, b % 2
                W("vector", mark=("S", k, b))
                O("vector", lambda eng, b=b, m=m, n=n: eng.tensor_tensor(
                    s_sb[:, m * D + n * 512: m * D + n * 512 + 512],
                    pb[b % 2][:, 0:512],
                    xnb[:, m * D + n * 512: m * D + n * 512 + 512], ALU.add),
                    inc="s_v", mark=("sevac", k, b))
            if k < NT:
                lnchain(k + 1)
            else:
                for b in (3, 1, 5, 7):
                    m, n = b // 2, b % 2
                    W("vector", mark=("dsilu", k, b))
                    O("vector", lambda eng, b=b, m=m, n=n: eng.tensor_mul(
                        gslab[b % 4][:, :], gslab[b % 4][:, :],
                        s_sb[:, m * D + n * 512: m * D + n * 512 + 512]))
                    DR("vector")
                    O("vector", lambda eng, b=b, m=m, n=n: eng.tensor_add(
                        xbuf[k % 2][:, m * D + n * 512: m * D + n * 512 + 512],
                        xbuf[k % 2][:, m * D + n * 512: m * D + n * 512 + 512],
                        gslab[b % 4][:, :]),
                        inc="s_v", mark=("comb", k, b))
                    DR("vector")

        def sync_tile(k):
            if k == 0:
                return
            for g in range(NGU):
                G = (k - 1) * NGU + g
                if G >= 3:
                    k2, g2 = (G - 3) // NGU + 1, (G - 3) % NGU
                    W("sync", mark=("umm", k2, 2 * g2 + 1))
                src = wgu_t[:, g * 512:(g + 1) * 512].rearrange(
                    "(kc p) h -> p kc h", p=128)
                dst = wgubuf[G % 3][:, :].rearrange("p (kc h) -> p kc h", h=512)
                O("sync", lambda eng, s=src, d=dst: eng.dma_start(
                    d, s), inc=("s_wgu0", "s_wgu1", "s_wgu2")[G % 3], n=16,
                    mark=("wgu", G))

        def wd_load(k):
            for g in range(4):
                src = wd_t[g * 768:(g + 1) * 768, :].rearrange(
                    "(c p) d2 -> p c d2", p=128)
                dst = wdfull[:, g * 6 * D:(g + 1) * 6 * D].rearrange(
                    "p (c d2) -> p c d2", d2=D)
                O("gpsimd", lambda eng, s=src, d=dst: eng.dma_start(d, s),
                  inc="s_wdf", n=16, mark=None if g < 3 else ("wdload", k))

        def gpsimd_tile(k):
            if k == 0:
                wd_load(1)
                W("gpsimd", mark=("LN", 0))
                W("gpsimd", mark=("lnchain", 0))
                x_load(2)
                return
            p = k % 2
            for b in ((2, 3, 0, 1, 4, 5, 6, 7) if k < NT else (2, 0, 4, 6)):
                m, n = b // 2, b % 2
                W("gpsimd", mark=("dsilu", k, b))
                O("gpsimd", lambda eng, b=b, m=m, n=n: eng.tensor_mul(
                    gslab[b % 4][:, :], gslab[b % 4][:, :],
                    s_sb[:, m * D + n * 512: m * D + n * 512 + 512]))
                DR("gpsimd")
                O("gpsimd", lambda eng, b=b, m=m, n=n, p=p: eng.tensor_add(
                    xbuf[p][:, m * D + n * 512: m * D + n * 512 + 512],
                    xbuf[p][:, m * D + n * 512: m * D + n * 512 + 512],
                    gslab[b % 4][:, :]),
                    inc="s_g", mark=("comb", k, b))
                DR("gpsimd")
                if b % 2 == 1 or k == NT:
                    if k == NT:
                        W("gpsimd", mark=("comb", k, b + 1))
                    src = xbuf[p][:, m * D:(m + 1) * D]
                    dst = out_h[(k - 1) * T + m * 128: (k - 1) * T + m * 128 + 128, :]
                    O("gpsimd", lambda eng, s=src, d=dst: eng.dma_start(d, s),
                      inc=f"s_out{k % 2}", n=16,
                      mark=("store", k) if m == 3 else None)
                    if m == 0 and k < NT:
                        W("gpsimd", mark=("down_m", k, 3))
                        wd_load(k + 1)
            if k + 2 <= NT:
                W("gpsimd", mark=("store", k))
                x_load(k + 2)

        scalar_sqrt(0)
        scalar_ln_apply(0)
        W("scalar", "s_seta", value=NSETUP_A)
        O("scalar", lambda eng: eng.activation(lam_sb[:, :], ll_sb[:, :], AF.Exp))
        W("scalar", "s_sets", value=NSETUP_S)
        O("scalar", lambda eng: eng.copy(bout_bf[:, :], bout_sb[:, :]),
          inc="s_a", mark=("setupA",))
        DR("scalar")

        for k in range(NT + 1):
            tensor_tile(k)
            scalar_tile(k)
            vector_tile(k)
            sync_tile(k)
            gpsimd_tile(k)
        W("gpsimd", mark=("store", NT))

    sc.recording = True
    walk()
    sc.recording = False
    sc.cnt.clear()
    walk()

    def play(eng, name):
        for item in sc.ops[name]:
            if item[0] == "op":
                _, make, inc, n = item
                ins = make(eng)
                if inc is not None:
                    ins.then_inc(sems[inc], n)
            else:
                _, sem, v = item
                eng.wait_ge(sems[sem], v)

    with nc.Block() as block:
        @block.tensor
        def _(eng):
            play(eng, "tensor")

        @block.vector
        def _(eng):
            play(eng, "vector")

        @block.scalar
        def _(eng):
            play(eng, "scalar")

        @block.sync
        def _(eng):
            play(eng, "sync")

        @block.gpsimd
        def _(eng):
            play(eng, "gpsimd")

    ctx.close()
    return nc


def make_in_maps(inputs):
    x = np.ascontiguousarray(np.asarray(inputs["L0_attn_x"], dtype=np.float32))
    lam_log = np.asarray(inputs["L0_attn_lambda_log"], dtype=np.float32)
    gamma = np.asarray(inputs["L0_attn_gamma"], dtype=np.float32)
    w_in = np.asarray(inputs["L0_attn_W_in"], dtype=np.float32)
    b_in = np.asarray(inputs["L0_attn_b_in"], dtype=np.float32)
    w_out = np.asarray(inputs["L0_attn_W_out"], dtype=np.float32)
    b_out = np.asarray(inputs["L0_attn_b_out"], dtype=np.float32)
    wg = np.asarray(inputs["L0_ffn_w_gate"])
    wu = np.asarray(inputs["L0_ffn_w_up"])
    wd = np.asarray(inputs["L0_ffn_w_down"])

    bf = ml_dtypes.bfloat16
    shared = {
        "win_t": np.ascontiguousarray(w_in.T),
        "wout_t": np.ascontiguousarray(w_out.T),
        "wgu_t": np.ascontiguousarray(np.concatenate(
            [np.ascontiguousarray(wg.T).reshape(D, NGU, 256),
             np.ascontiguousarray(wu.T).reshape(D, NGU, 256)],
            axis=2).reshape(D, 2 * H)).astype(bf),
        "wd_t": np.ascontiguousarray(wd.T).astype(bf),
        "ll": np.ascontiguousarray(lam_log.reshape(NK, 128).T),
        "gam": np.ascontiguousarray(gamma.reshape(NK, 128).T),
        "bin": np.ascontiguousarray(b_in.reshape(NK, 128).T),
        "bout": np.ascontiguousarray(b_out.reshape(1, D)),
        "ident": np.eye(128, dtype=np.float32),
        "ones1": np.ones((1, 128), dtype=bf),
        "epsv": np.full((128, 1), 1e-5, dtype=np.float32),
    }
    in_maps = []
    for core in range(8):
        b, q = core // 4, core % 4
        own = x[b, q * 4096:(q + 1) * 4096, :]
        if q > 0:
            warm = x[b, q * 4096 - WARM: q * 4096, :]
            mask = np.ones((128, 1), dtype=np.float32)
        else:
            warm = x[b, 0:WARM, :]
            mask = np.zeros((128, 1), dtype=np.float32)
        m = dict(shared)
        m["xe"] = np.ascontiguousarray(np.concatenate([warm, own], axis=0))
        m["mask"] = mask
        in_maps.append(m)
    return in_maps


def kernel(**inputs):
    nc = build_nc()
    in_maps = make_in_maps(inputs)
    res = run_bass_kernel_spmd(nc, in_maps, core_ids=list(range(8)))
    out = np.empty((2, 16384, D), dtype=np.float32)
    for core in range(8):
        b, q = core // 4, core % 4
        out[b, q * 4096:(q + 1) * 4096, :] = res.results[core]["out"]
    return out


# version 27
# speedup vs baseline: 1.0057x; 1.0030x over previous
import contextlib
from collections import defaultdict

import numpy as np
import ml_dtypes

import concourse.bass as bass
import concourse.mybir as mybir
from concourse.bass_utils import run_bass_kernel_spmd

F32 = mybir.dt.float32
BF16 = mybir.dt.bfloat16
AF = mybir.ActivationFunctionType
ALU = mybir.AluOpType

D = 1024
H = 3072
T = 512
NT = 8
NK = D // 128
NHC = H // 128
NGU = H // 256
WARM = 256
QINV = 1.0 / 127.0
NSETUP_G = 16 * 2
NSETUP_A = 16 * 3
NSETUP_S = 16 * 5


class Sched:

    def __init__(self):
        self.recording = True
        self.cnt = defaultdict(int)
        self.marks = {}
        self.ops = defaultdict(list)

    def op(self, engine, make, inc=None, n=1, mark=None):
        if self.recording:
            if inc is not None:
                self.cnt[inc] += n
            if mark is not None:
                assert mark not in self.marks, f"dup mark {mark}"
                assert inc is not None, f"mark {mark} needs inc"
                self.marks[mark] = (inc, self.cnt[inc])
        else:
            self.ops[engine].append(("op", make, inc, n))

    def wait(self, engine, sem=None, mark=None, value=None):
        if self.recording:
            return
        if mark is not None:
            msem, v = self.marks[mark]
            assert sem is None or sem == msem, f"sem mismatch at {mark}"
            sem = msem
        else:
            v = value
        self.ops[engine].append(("wait", sem, v))

    def drain(self, engine):
        self.op(engine, lambda eng: eng.drain())


def build_nc():
    nc = bass.Bass("TRN2", target_bir_lowering=False)

    di = lambda name, sh, dt=F32: nc.dram_tensor(name, sh, dt, kind="ExternalInput").ap()
    xe = di("xe", [NT * T + WARM, D])
    win_t = di("win_t", [D, D])
    wout_t = di("wout_t", [D, D])
    wgu_t = di("wgu_t", [D, 2 * H], BF16)
    wd_t = di("wd_t", [H, D], BF16)
    ll_h = di("ll", [128, NK])
    gam_h = di("gam", [128, NK])
    bin_h = di("bin", [128, NK])
    bout_h = di("bout", [1, D])
    mask_h = di("mask", [128, 1])
    ident_h = di("ident", [128, 128])
    ones_h = di("ones1", [1, 128], BF16)
    out_h = nc.dram_tensor("out", [NT * T, D], F32, kind="ExternalOutput").ap()

    ctx = contextlib.ExitStack()
    sb = lambda name, sh, dt: ctx.enter_context(nc.sbuf_tensor(name, sh, dt))

    winT = sb("winT", [128, NK * D], BF16)
    woutT = sb("woutT", [128, NK * D], BF16)
    wgubuf = [sb(f"wgubuf{i}", [128, NK * 512], BF16) for i in range(3)]
    wdfull = sb("wdfull", [128, NHC * D], BF16)
    xbuf = [sb(f"xbuf{i}", [128, 4 * D], F32) for i in range(2)]
    xnb = sb("xnb", [128, 4 * D], F32)
    xnbT = sb("xnbT", [128, NK * T], BF16)
    guT = sb("guT", [128, NK * T], BF16)
    hTb = guT
    carry = sb("carry", [128, NK], F32)
    s_sb = sb("s_sb", [128, 4 * D], BF16)
    hbT = sb("hbT", [128, NHC * T], BF16)
    gslab = [sb(f"gslab{i}", [128, T], BF16) for i in range(4)]
    siluT = gslab[:2]
    stats = sb("stats", [128, 48], F32)
    mv = sb("mv", [128, 8], F32)
    sq4 = sb("sq4", [128, 4], F32)
    rstd4 = sb("rstd4", [128, 4], F32)
    negmr4 = sb("negmr4", [128, 4], F32)
    lam_sb = sb("lam_sb", [128, NK], F32)
    ll_sb = sb("ll_sb", [128, NK], F32)
    gam_sb = sb("gam_sb", [128, NK], F32)
    bin_sb = sb("bin_sb", [128, NK], F32)
    bgam_sb = sb("bgam_sb", [128, NK], F32)
    bout_bf = sb("bout_bf", [1, D], BF16)
    mask_sb = sb("mask_sb", [128, 1], F32)
    ident_sb = sb("ident_sb", [128, 128], F32)
    ones_sb = sb("ones_sb", [1, 128], BF16)
    bout_sb = xnb[0:1, 3 * D:4 * D]
    eps_h = di_eps = nc.dram_tensor("epsv", [128, 1], F32,
                                    kind="ExternalInput").ap()
    eps_sb = sb("eps_sb", [128, 1], F32)
    dummy_sb = sb("dummy_sb", [128, 1], F32)

    pb = [ctx.enter_context(nc.psum_tensor(f"pb{i}", [128, 512], F32))
          for i in range(8)]

    sems = {}
    for s in ("s_x0", "s_x1", "s_out0", "s_out1", "s_wgu0", "s_wgu1",
              "s_wdf", "s_setg", "s_sets", "s_t", "s_v", "s_a", "s_g",
              "s_xh0", "s_xh1", "s_xh2", "s_seta", "s_wgu2"):
        sems[s] = ctx.enter_context(nc.semaphore(s))

    sc = Sched()

    def walk():
        W = sc.wait
        O = sc.op
        DR = sc.drain

        def x_load(k, engine="gpsimd", sem=None):
            r0 = WARM + (k - 1) * T
            src = xe[r0:r0 + T, :].rearrange("(j p) c -> p j c", p=128)
            dst = xbuf[k % 2][:, :].rearrange("p (j c) -> p j c", c=D)
            O(engine, lambda eng, s=src, d=dst: eng.dma_start(d, s),
              inc=sem or f"s_x{k % 2}", n=16, mark=("x", k))

        def x0_half(half, sem):
            src = xe[half * 128:(half + 1) * 128, :]
            dst = xbuf[0][:, half * D:(half + 1) * D]
            O("sync", lambda eng, s=src, d=dst: eng.dma_start(d, s),
              inc=sem, n=16, mark=("x0h", half) if half == 0 else ("x", 0))

        x0_half(0, "s_xh0")
        for src, dst in ((eps_h, eps_sb), (ll_h, ll_sb), (ident_h, ident_sb)):
            O("sync", lambda eng, s=src, d=dst: eng.dma_start(d[:, :], s[:, :]),
              inc="s_seta", n=16)
        x0_half(1, "s_xh1")
        x_load(1, engine="sync", sem="s_xh2")
        for src, dst in ((gam_h, gam_sb), (bin_h, bin_sb), (bout_h, bout_sb),
                         (mask_h, mask_sb), (ones_h, ones_sb)):
            O("sync", lambda eng, s=src, d=dst: eng.dma_start(d[:, :], s[:, :]),
              inc="s_sets", n=16)
        for h in range(2):
            src_w = win_t[h * 512:(h + 1) * 512, :].rearrange(
                "(kc p) e -> p kc e", p=128)
            dst_w = winT[:, h * 4 * D:(h + 1) * 4 * D].rearrange(
                "p (kc e) -> p kc e", e=D)
            O("gpsimd", lambda eng, s=src_w, d=dst_w: eng.dma_start(d, s),
              inc="s_setg", n=16, mark=("winh", h))
            W("gpsimd", mark=("winh", h))
        src_w2 = wout_t.rearrange("(kc p) e -> p kc e", p=128)
        dst_w2 = woutT[:, :].rearrange("p (kc e) -> p kc e", e=D)
        O("gpsimd", lambda eng, s=src_w2, d=dst_w2: eng.dma_start(d, s),
          inc="s_setg", n=16, mark=("wcast", 1))



        W("tensor", "s_seta", value=NSETUP_A)


        def lnchain_a(k):
            p = k % 2
            if k == 0:
                W("vector", mark=("x0h", 0))
            else:
                W("vector", mark=("x", k))
            DR("vector")
            nj = 2 if k == 0 else 4
            for j in range(nj):
                if k == 0 and j == 1:
                    W("vector", mark=("x", 0))
                for h in range(2):
                    O("vector", lambda eng, j=j, h=h, p=p: eng.bn_stats(
                        stats[:, j * 12 + 6 * h: j * 12 + 6 * h + 6],
                        xbuf[p][:, j * D + 512 * h: j * D + 512 * h + 512]))
            DR("vector")
            for j in range(nj):
                O("vector", lambda eng, j=j: eng.bn_aggr(
                    mv[:, 2 * j:2 * j + 2], stats[:, j * 12:(j + 1) * 12]),
                    inc="s_v", mark=None if j < nj - 1 else ("aggr", k))

        def lnchain_b(k):
            nj = 2 if k == 0 else 4
            W("vector", mark=("sqrt", k))
            O("vector", lambda eng, nj=nj: eng.reciprocal(rstd4[:, 0:nj],
                                                          sq4[:, 0:nj]))
            DR("vector")
            O("vector", lambda eng, nj=nj: eng.scalar_tensor_tensor(
                negmr4[:, 0:nj], mv[:, 0:2 * nj:2], -1.0, rstd4[:, 0:nj],
                ALU.mult, ALU.mult),
                inc="s_v", mark=("lnchain", k))

        def lnchain(k):
            lnchain_a(k)
            lnchain_b(k)

        def scalar_sqrt(k):
            if k == 0:
                W("scalar", "s_seta", value=NSETUP_A)
            W("scalar", mark=("aggr", k))
            if k >= 1:
                W("scalar", mark=("lnchain", k - 1))
            DR("scalar")
            nj = 2 if k == 0 else 4
            O("scalar", lambda eng, nj=nj: eng.activation(
                sq4[:, 0:nj], mv[:, 1:2 * nj:2], AF.Sqrt, bias=eps_sb[:, 0:1]),
                inc="s_a", mark=("sqrt", k))

        def scalar_ln_apply(k):
            p = k % 2
            W("scalar", mark=("lnchain", k))
            W("scalar", mark=("x", k))
            if k == 0:
                W("scalar", mark=("x0h", 0))
            if k >= 2:
                W("scalar", mark=("sevac", k - 1, 7))
            elif k == 1:
                W("scalar", mark=("T", 0, 7))
            nj = 2 if k == 0 else 4
            for j in range(nj):
                O("scalar", lambda eng, j=j, p=p: eng.activation(
                    xnb[:, j * D:(j + 1) * D], xbuf[p][:, j * D:(j + 1) * D],
                    AF.Identity, bias=negmr4[:, j:j + 1], scale=rstd4[:, j:j + 1]),
                    inc="s_a", mark=None if j < nj - 1 else ("LN", k))
            O("scalar", lambda eng: eng.activation(dummy_sb[:, :], eps_sb[:, :],
                                                   AF.Silu))

        def tensor_tile(k):
            real = k >= 1
            W("tensor", mark=("LN", k))
            jw = 2 if k == 0 else 4
            for kc in range(NK):
                if kc >= 4:
                    W("tensor", mark=("Tevac", k, kc - 4))
                elif k >= 2:
                    W("tensor", mark=("dsilu", k - 1, kc))
                elif k == 1 and kc < 2:
                    W("tensor", mark=("guevac", 0, 6 + kc))
                for j in range(jw):
                    O("tensor", lambda eng, kc=kc, j=j: eng.transpose(
                        pb[kc % 4][:, j * 128:(j + 1) * 128],
                        xnb[:, j * D + kc * 128: j * D + kc * 128 + 128],
                        ident_sb[:, :]),
                        inc=None if j < jw - 1 else "s_t",
                        mark=None if j < jw - 1 else ("T", k, kc))
            for ec in range(NK):
                if ec == 0:
                    W("tensor", mark=("Tevac", k, 4))
                elif ec == 1:
                    W("tensor", mark=("Tevac", k, 5))
                else:
                    W("tensor", mark=("guevac", k, ec - 2))
                tw = WARM if k == 0 else T
                for kc in range(NK):
                    if ec == 0:
                        W("tensor", mark=("Tevac", k, kc))
                        if k == 0:
                            W("tensor", mark=("winh", kc // 4))
                    O("tensor", lambda eng, ec=ec, kc=kc, tw=tw: eng.matmul(
                        pb[ec % 2][:, 0:tw],
                        winT[:, kc * D + ec * 128: kc * D + ec * 128 + 128],
                        xnbT[:, kc * T: kc * T + tw],
                        start=(kc == 0), stop=(kc == NK - 1)),
                        inc=None if kc < NK - 1 else "s_t",
                        mark=None if kc < NK - 1 else ("U", k, ec))
            if not real:
                return
            for g in range(NGU):
                G = (k - 1) * NGU + g
                W("tensor", mark=("wgu", G))
                for cc in range(2):
                    c = 2 * g + cc
                    gb = 2 + 2 * (c % 3)
                    ub = 3 + 2 * (c % 3)
                    if c >= 3:
                        W("tensor", mark=("gsilu", k, c - 3))
                        W("tensor", mark=("gumul", k, c - 3))
                    elif k >= 2:
                        W("tensor", "s_a",
                          mark=("dsilu", k - 1, 3 + 2 * c))
                    for kc in range(NK):
                        O("tensor", lambda eng, g=g, cc=cc, kc=kc, gb=gb: eng.matmul(
                            pb[gb][:, 0:512],
                            wgubuf[g % 3][:, kc * 512 + cc * 128: kc * 512 + cc * 128 + 128],
                            xnbT[:, kc * T:(kc + 1) * T],
                            start=(kc == 0), stop=(kc == NK - 1)),
                            inc=None if kc < NK - 1 else "s_t",
                            mark=None if kc < NK - 1 else ("gmm", k, c))
                    for kc in range(NK):
                        O("tensor", lambda eng, g=g, cc=cc, kc=kc, ub=ub: eng.matmul(
                            pb[ub][:, 0:512],
                            wgubuf[g % 3][:, kc * 512 + 256 + cc * 128: kc * 512 + 256 + cc * 128 + 128],
                            xnbT[:, kc * T:(kc + 1) * T],
                            start=(kc == 0), stop=(kc == NK - 1)),
                            inc=None if kc < NK - 1 else "s_t",
                            mark=None if kc < NK - 1 else ("umm", k, c))
            if k == 1:
                W("tensor", mark=("wcast", 1))
                W("tensor", "s_sets", value=NSETUP_S)
            W("tensor", mark=("scanall", k))
            W("tensor", mark=("guevac", k, NK - 1))
            for b in range(8):
                m, n = b // 2, b % 2
                if b >= 2:
                    W("tensor", mark=("sevac", k, b - 2))
                O("tensor", lambda eng, b=b, n=n: eng.matmul(
                    pb[b % 2][:, 0:512], ones_sb[0:1, :],
                    bout_bf[0:1, n * 512:(n + 1) * 512],
                    start=True, stop=False))
                for kc in range(NK):
                    O("tensor", lambda eng, b=b, m=m, n=n, kc=kc: eng.matmul(
                        pb[b % 2][:, 0:512],
                        hTb[:, kc * T + m * 128: kc * T + m * 128 + 128],
                        woutT[:, kc * D + n * 512: kc * D + n * 512 + 512],
                        start=False, stop=(kc == NK - 1)),
                        inc=None if kc < NK - 1 else "s_t",
                        mark=None if kc < NK - 1 else ("S", k, b))
            W("tensor", mark=("gumul", k, NHC - 1))
            W("tensor", mark=("gsilu", k, NHC - 1))
            W("tensor", mark=("wdload", k))
            for m in (1, 0, 2, 3):
                if m == 0:
                    W("tensor", mark=("sevac", k, 7))
                if k >= 2:
                    W("tensor", mark=("dsilu", k - 1, 2 * m + 1))
                for c in range(NHC):
                    for n in range(2):
                        last = (c == NHC - 1 and n == 1)
                        O("tensor", lambda eng, c=c, m=m, n=n: eng.matmul(
                            pb[2 * m + n][:, 0:512],
                            hbT[:, c * T + m * 128: c * T + m * 128 + 128],
                            wdfull[:, c * D + n * 512: c * D + n * 512 + 512],
                            start=(c == 0), stop=(c == NHC - 1)),
                            inc="s_t" if last else None,
                            mark=("down_m", k, m) if last else None)

        def dsilu_emit(k, bs):
            for b in bs:
                W("scalar", mark=("down_m", k, b // 2))
                if b >= 4:
                    W("scalar", mark=("comb", k, b - 4))
                elif k >= 2:
                    W("scalar", mark=("comb", k - 1, 4 + b))
                O("scalar", lambda eng, b=b: eng.activation(
                    gslab[b % 4][:, :], pb[b][:, 0:512], AF.Silu, scale=QINV),
                    inc="s_a", mark=("dsilu", k, b))

        def scalar_tile(k):
            real = k >= 1
            if real:
                if k >= 2:
                    W("scalar", mark=("umm", k - 1, NHC - 1))
                else:
                    W("scalar", mark=("U", 0, NK - 1))
            tw = WARM if k == 0 else T
            for kc in range(0, NK, 2):
                W("scalar", mark=("T", k, kc))
                O("scalar", lambda eng, kc=kc, tw=tw: eng.copy(
                    xnbT[:, kc * T: kc * T + tw], pb[kc % 4][:, 0:tw]),
                    inc="s_a", mark=("Tevac", k, kc))
            if k >= 2:
                dsilu_emit(k - 1, (4, 5, 6, 7))
            if k == 0:
                scalar_sqrt(1)
            if k == 0:
                W("scalar", mark=("setupV",))
            if k >= 1:
                W("scalar", mark=("carry", k - 1))
            if k >= 2:
                W("scalar", mark=("S", k - 1, 7))
            for ec in range(NK):
                W("scalar", mark=("U", k, ec))
                O("scalar", lambda eng, ec=ec, tw=tw: eng.activation(
                    guT[:, ec * T: ec * T + tw], pb[ec % 2][:, 0:tw],
                    AF.Identity, bias=bgam_sb[:, ec:ec + 1],
                    scale=gam_sb[:, ec:ec + 1]),
                    inc="s_a", mark=("guevac", k, ec))
            if not real:
                W("scalar", "s_sets", value=NSETUP_S)
                O("scalar", lambda eng: eng.copy(bout_bf[:, :], bout_sb[:, :]),
                  inc="s_a", mark=("setupA",))
                DR("scalar")
                scalar_ln_apply(1)
                return
            for c in range(NHC):
                W("scalar", mark=("gmm", k, c))
                if c >= 2:
                    W("scalar", mark=("gumul", k, c - 2))
                elif k >= 2:
                    W("scalar", mark=("comb", k - 1, 4 + c))
                O("scalar", lambda eng, c=c: eng.activation(
                    siluT[c % 2][:, :], pb[2 + 2 * (c % 3)][:, 0:512],
                    AF.Silu, scale=QINV),
                    inc="s_a", mark=("gsilu", k, c))
            if k < NT:
                scalar_sqrt(k + 1)
                scalar_ln_apply(k + 1)
            dsilu_emit(k, (2, 3, 0, 1))
            if k == NT:
                dsilu_emit(k, (4, 5, 6, 7))

        def vector_tile(k):
            real = k >= 1
            if k == 0:
                lnchain(0)
                W("vector", "s_sets", value=NSETUP_S)
                O("vector", lambda eng: eng.tensor_mul(
                    bgam_sb[:, :], bin_sb[:, :], gam_sb[:, :]),
                    inc="s_v", mark=("setupV",))
                DR("vector")
                lnchain_a(1)
            if k >= 2:
                W("vector", mark=("umm", k - 1, NHC - 1))
            elif k == 1:
                W("vector", mark=("U", 0, NK - 1))
            tw = WARM if k == 0 else T
            for kc in range(1, NK, 2):
                W("vector", mark=("T", k, kc))
                O("vector", lambda eng, kc=kc, tw=tw: eng.tensor_copy(
                    xnbT[:, kc * T: kc * T + tw], pb[kc % 4][:, 0:tw]),
                    inc="s_v", mark=("Tevac", k, kc))
            def scan_emit(ec):
                W("vector", mark=("guevac", k, ec))
                init = 0.0 if k == 0 else carry[:, ec:ec + 1]
                O("vector", lambda eng, ec=ec, init=init, tw=tw:
                    eng.tensor_tensor_scan(
                        hTb[:, ec * T: ec * T + tw],
                        lam_sb[:, ec:ec + 1].broadcast_to([128, tw]),
                        guT[:, ec * T: ec * T + tw],
                        init, ALU.mult, ALU.add),
                    inc="s_v", mark=None if ec < NK - 1 else ("scanall", k))

            def carry_emit():
                DR("vector")
                for ec in range(NK):
                    O("vector", lambda eng, ec=ec, tw=tw: eng.tensor_copy(
                        carry[:, ec:ec + 1],
                        hTb[:, ec * T + tw - 1: ec * T + tw]),
                        inc=None if ec < NK - 1 else "s_v",
                        mark=None if ec < NK - 1 else ("carry", k))

            DR("vector")
            if k >= 2:
                W("vector", mark=("S", k - 1, 7))
            if k == 0:
                lnchain_b(1)
                for ec in range(NK):
                    scan_emit(ec)
                carry_emit()
                DR("vector")
                O("vector", lambda eng: eng.tensor_scalar_mul(
                    carry[:, :], carry[:, :], mask_sb[:, 0:1]))
                DR("vector")
                return
            if k >= 2:
                W("vector", mark=("down_m", k - 1, 3))
            for c in range(NHC):
                W("vector", mark=("umm", k, c))
                W("vector", mark=("gsilu", k, c))
                O("vector", lambda eng, c=c: eng.scalar_tensor_tensor(
                    hbT[:, c * T:(c + 1) * T], pb[3 + 2 * (c % 3)][:, 0:512],
                    QINV, siluT[c % 2][:, :], ALU.mult, ALU.mult),
                    inc="s_v", mark=("gumul", k, c))
                if c % 3 == 1:
                    scan_emit(c // 3)
            carry_emit()
            DR("vector")
            if k >= 2:
                W("vector", mark=("comb", k - 1, 7))
            for b in range(8):
                m, n = b // 2, b % 2
                W("vector", mark=("S", k, b))
                O("vector", lambda eng, b=b, m=m, n=n: eng.tensor_tensor(
                    s_sb[:, m * D + n * 512: m * D + n * 512 + 512],
                    pb[b % 2][:, 0:512],
                    xnb[:, m * D + n * 512: m * D + n * 512 + 512], ALU.add),
                    inc="s_v", mark=("sevac", k, b))
            if k < NT:
                lnchain(k + 1)
            else:
                for b in (3, 1, 5, 7):
                    m, n = b // 2, b % 2
                    W("vector", mark=("dsilu", k, b))
                    O("vector", lambda eng, b=b, m=m, n=n: eng.tensor_mul(
                        gslab[b % 4][:, :], gslab[b % 4][:, :],
                        s_sb[:, m * D + n * 512: m * D + n * 512 + 512]))
                    DR("vector")
                    O("vector", lambda eng, b=b, m=m, n=n: eng.tensor_add(
                        xbuf[k % 2][:, m * D + n * 512: m * D + n * 512 + 512],
                        xbuf[k % 2][:, m * D + n * 512: m * D + n * 512 + 512],
                        gslab[b % 4][:, :]),
                        inc="s_v", mark=("comb", k, b))
                    DR("vector")

        def sync_tile(k):
            if k == 0:
                return
            for g in range(NGU):
                G = (k - 1) * NGU + g
                if G >= 3:
                    k2, g2 = (G - 3) // NGU + 1, (G - 3) % NGU
                    W("sync", mark=("umm", k2, 2 * g2 + 1))
                src = wgu_t[:, g * 512:(g + 1) * 512].rearrange(
                    "(kc p) h -> p kc h", p=128)
                dst = wgubuf[G % 3][:, :].rearrange("p (kc h) -> p kc h", h=512)
                O("sync", lambda eng, s=src, d=dst: eng.dma_start(
                    d, s), inc=("s_wgu0", "s_wgu1", "s_wgu2")[G % 3], n=16,
                    mark=("wgu", G))

        def wd_load(k):
            for g in range(4):
                src = wd_t[g * 768:(g + 1) * 768, :].rearrange(
                    "(c p) d2 -> p c d2", p=128)
                dst = wdfull[:, g * 6 * D:(g + 1) * 6 * D].rearrange(
                    "p (c d2) -> p c d2", d2=D)
                O("gpsimd", lambda eng, s=src, d=dst: eng.dma_start(d, s),
                  inc="s_wdf", n=16, mark=None if g < 3 else ("wdload", k))

        def gpsimd_tile(k):
            if k == 0:
                wd_load(1)
                W("gpsimd", mark=("LN", 0))
                W("gpsimd", mark=("lnchain", 0))
                x_load(2)
                return
            p = k % 2
            for b in ((2, 3, 0, 1, 4, 5, 6, 7) if k < NT else (2, 0, 4, 6)):
                m, n = b // 2, b % 2
                W("gpsimd", mark=("dsilu", k, b))
                O("gpsimd", lambda eng, b=b, m=m, n=n: eng.tensor_mul(
                    gslab[b % 4][:, :], gslab[b % 4][:, :],
                    s_sb[:, m * D + n * 512: m * D + n * 512 + 512]))
                DR("gpsimd")
                O("gpsimd", lambda eng, b=b, m=m, n=n, p=p: eng.tensor_add(
                    xbuf[p][:, m * D + n * 512: m * D + n * 512 + 512],
                    xbuf[p][:, m * D + n * 512: m * D + n * 512 + 512],
                    gslab[b % 4][:, :]),
                    inc="s_g", mark=("comb", k, b))
                DR("gpsimd")
                if b % 2 == 1 or k == NT:
                    if k == NT:
                        W("gpsimd", mark=("comb", k, b + 1))
                    src = xbuf[p][:, m * D:(m + 1) * D]
                    dst = out_h[(k - 1) * T + m * 128: (k - 1) * T + m * 128 + 128, :]
                    O("gpsimd", lambda eng, s=src, d=dst: eng.dma_start(d, s),
                      inc=f"s_out{k % 2}", n=16,
                      mark=("store", k) if m == 3 else None)
                    if m == 0 and k < NT:
                        W("gpsimd", mark=("down_m", k, 3))
                        wd_load(k + 1)
            if k + 2 <= NT:
                W("gpsimd", mark=("store", k))
                x_load(k + 2)

        scalar_sqrt(0)
        scalar_ln_apply(0)
        W("scalar", "s_seta", value=NSETUP_A)
        O("scalar", lambda eng: eng.activation(lam_sb[:, :], ll_sb[:, :], AF.Exp))
        DR("scalar")

        for k in range(NT + 1):
            tensor_tile(k)
            scalar_tile(k)
            vector_tile(k)
            sync_tile(k)
            gpsimd_tile(k)
        W("gpsimd", mark=("store", NT))

    sc.recording = True
    walk()
    sc.recording = False
    sc.cnt.clear()
    walk()

    def play(eng, name):
        for item in sc.ops[name]:
            if item[0] == "op":
                _, make, inc, n = item
                ins = make(eng)
                if inc is not None:
                    ins.then_inc(sems[inc], n)
            else:
                _, sem, v = item
                eng.wait_ge(sems[sem], v)

    with nc.Block() as block:
        @block.tensor
        def _(eng):
            play(eng, "tensor")

        @block.vector
        def _(eng):
            play(eng, "vector")

        @block.scalar
        def _(eng):
            play(eng, "scalar")

        @block.sync
        def _(eng):
            play(eng, "sync")

        @block.gpsimd
        def _(eng):
            play(eng, "gpsimd")

    ctx.close()
    return nc


def make_in_maps(inputs):
    x = np.ascontiguousarray(np.asarray(inputs["L0_attn_x"], dtype=np.float32))
    lam_log = np.asarray(inputs["L0_attn_lambda_log"], dtype=np.float32)
    gamma = np.asarray(inputs["L0_attn_gamma"], dtype=np.float32)
    w_in = np.asarray(inputs["L0_attn_W_in"], dtype=np.float32)
    b_in = np.asarray(inputs["L0_attn_b_in"], dtype=np.float32)
    w_out = np.asarray(inputs["L0_attn_W_out"], dtype=np.float32)
    b_out = np.asarray(inputs["L0_attn_b_out"], dtype=np.float32)
    wg = np.asarray(inputs["L0_ffn_w_gate"])
    wu = np.asarray(inputs["L0_ffn_w_up"])
    wd = np.asarray(inputs["L0_ffn_w_down"])

    bf = ml_dtypes.bfloat16
    shared = {
        "win_t": np.ascontiguousarray(w_in.T),
        "wout_t": np.ascontiguousarray(w_out.T),
        "wgu_t": np.ascontiguousarray(np.concatenate(
            [np.ascontiguousarray(wg.T).reshape(D, NGU, 256),
             np.ascontiguousarray(wu.T).reshape(D, NGU, 256)],
            axis=2).reshape(D, 2 * H)).astype(bf),
        "wd_t": np.ascontiguousarray(wd.T).astype(bf),
        "ll": np.ascontiguousarray(lam_log.reshape(NK, 128).T),
        "gam": np.ascontiguousarray(gamma.reshape(NK, 128).T),
        "bin": np.ascontiguousarray(b_in.reshape(NK, 128).T),
        "bout": np.ascontiguousarray(b_out.reshape(1, D)),
        "ident": np.eye(128, dtype=np.float32),
        "ones1": np.ones((1, 128), dtype=bf),
        "epsv": np.full((128, 1), 1e-5, dtype=np.float32),
    }
    in_maps = []
    for core in range(8):
        b, q = core // 4, core % 4
        own = x[b, q * 4096:(q + 1) * 4096, :]
        if q > 0:
            warm = x[b, q * 4096 - WARM: q * 4096, :]
            mask = np.ones((128, 1), dtype=np.float32)
        else:
            warm = x[b, 0:WARM, :]
            mask = np.zeros((128, 1), dtype=np.float32)
        m = dict(shared)
        m["xe"] = np.ascontiguousarray(np.concatenate([warm, own], axis=0))
        m["mask"] = mask
        in_maps.append(m)
    return in_maps


def kernel(**inputs):
    nc = build_nc()
    in_maps = make_in_maps(inputs)
    res = run_bass_kernel_spmd(nc, in_maps, core_ids=list(range(8)))
    out = np.empty((2, 16384, D), dtype=np.float32)
    for core in range(8):
        b, q = core // 4, core % 4
        out[b, q * 4096:(q + 1) * 4096, :] = res.results[core]["out"]
    return out
